# version 07 (final)
# baseline (speedup 1.0000x reference)
import contextlib
import numpy as np
import concourse.bass as bass
import concourse.mybir as mybir
from concourse.bass_utils import run_bass_kernel_spmd
from concourse.alu_op_type import AluOpType

F32 = mybir.dt.float32
BF16 = mybir.dt.bfloat16
AF = mybir.ActivationFunctionType

S, TOK, D, NH = 2048, 1024, 4096, 32
KTOP = 409
NDC = 32
NQT = TOK // 128
NKC = S // 128
NJ = 15
SQ = float(1.0 / np.sqrt(128.0))
EPS = 1e-5
WIN = 256
NWIN = S // WIN
NTILE = NH * NQT


def build_nc(stop=None, nheads=NH):
    nc = bass.Bass("TRN2", target_bir_lowering=False)
    x = nc.dram_tensor("x", [S, D], F32, kind="ExternalInput").ap()
    wqT = nc.dram_tensor("wqT", [NH, 128, NDC * 128], F32, kind="ExternalInput").ap()
    wkT = nc.dram_tensor("wkT", [NH, 128, NDC * 128], F32, kind="ExternalInput").ap()
    wvT = nc.dram_tensor("wvT", [NH, 128, NDC * 128], F32, kind="ExternalInput").ap()
    woT = nc.dram_tensor("woT", [32, 8, 128, 512], F32, kind="ExternalInput").ap()
    bWT = nc.dram_tensor("bWT", [32, 8, 128, 512], F32, kind="ExternalInput").ap()
    gWT = nc.dram_tensor("gWT", [128, NDC * 28], F32, kind="ExternalInput").ap()
    lAT = nc.dram_tensor("lAT", [2, NDC, 128, 128], F32, kind="ExternalInput").ap()
    lBT = nc.dram_tensor("lBT", [2, 128, 8, 512], F32, kind="ExternalInput").ap()
    bqc = nc.dram_tensor("bqc", [128, 32], F32, kind="ExternalInput").ap()
    bkc = nc.dram_tensor("bkc", [128, 32], F32, kind="ExternalInput").ap()
    bvc = nc.dram_tensor("bvc", [128, 32], F32, kind="ExternalInput").ap()
    boc = nc.dram_tensor("boc", [1, D], F32, kind="ExternalInput").ap()
    identd = nc.dram_tensor("identd", [128, 128], F32, kind="ExternalInput").ap()
    reptd = nc.dram_tensor("reptd", [28, 256], F32, kind="ExternalInput").ap()
    iotad = nc.dram_tensor("iotad", [128, 8], F32, kind="ExternalInput").ap()
    out = nc.dram_tensor("out", [TOK, D], F32, kind="ExternalOutput").ap()
    xnt = nc.dram_tensor("xnt", [NWIN, 128, NDC * WIN], BF16, kind="Internal").ap()
    ctxd = nc.dram_tensor("ctxd", [NH, 128, TOK], BF16, kind="Internal").ap()
    xmidd = nc.dram_tensor("xmidd", [8, 128, D], F32, kind="Internal").ap()

    cm = contextlib.ExitStack()
    sb = lambda n, s_, d: cm.enter_context(nc.sbuf_tensor(n, s_, d))
    pt = lambda n, s_: cm.enter_context(nc.psum_tensor(n, s_, F32))
    sem = lambda n: cm.enter_context(nc.semaphore(n))

    AFa = sb("AFa", [128, 16384], F32)
    BFa = sb("BFa", [128, 49152], BF16)
    BFb = sb("BFb", [128, 16384], BF16)
    VEC = sb("VEC", [128, 320], F32)
    ident = sb("ident", [128, 128], BF16)
    rept = sb("rept", [28, 256], BF16)
    coliota = sb("coliota", [128, 8], F32)
    bq_sb = sb("bq_sb", [128, 32], F32)
    bk_sb = sb("bk_sb", [128, 32], F32)
    bv_sb = sb("bv_sb", [128, 32], F32)
    bo_sb = sb("bo_sb", [1, D], BF16)
    ones1 = sb("ones1", [1, 128], BF16)
    VECU = sb("VECU", [128, 8], mybir.dt.uint32)

    fA = lambda o, n: AFa[:, o:o + n]
    fB = lambda o, n: BFa[:, o:o + n]
    fb = lambda o, n: BFb[:, o:o + n]

    sbufs = [fA(0, 2048), fA(2048, 2048)]
    zbuf = fA(4096, 2048)
    xc = [fA(0, 4096), fA(4096, 4096)]
    xmc = [fA(0, 4096), fA(4096, 4096)]
    xs_c = [fA(8192, 512), fA(8704, 512)]
    om = [fA(9216, 512), fA(9728, 512)]
    xr = [fA(10240, 512), fA(10752, 512)]
    lgs, l2s = fA(11264, 28), fA(11292, 28)
    e1s, e2s, gfs = fA(11320, 28), fA(11348, 28), fA(11376, 28)

    xw = [fB(0, 8192), fB(8192, 8192)]
    wbuf = [fB(16384, 4096), fB(20480, 4096), fB(24576, 4096)]
    kT = [fB(28672, 2048), fB(30720, 2048)]
    qT = [fB(32768, 1024), fB(33792, 1024)]
    vTb = fB(34816, 2048)
    vnat = [fB(36864, 2048), fB(38912, 2048)]
    junkA = fB(40960, 2048)
    junkD = fB(43008, 2048)
    ebuf = [fB(45056, 2048), fB(47104, 2048)]
    ctx = fB(0, 32768)
    ws = [fB(32768, 512), fB(33280, 512)]
    gwb = fB(33792, 896)
    lab = [fB(34816, 128), fB(34944, 128)]
    lbb = [fB(35840, 512), fB(36352, 512)]
    hT = fB(36864, 2048)
    gr = fB(38912, 2048)
    gtb = fB(40960, 128)
    gbb = fB(41088, 28)
    xn2 = fB(43008, 4096)
    pbuf = [fb(0, 2048), fb(2048, 2048)]
    pTg = fb(4096, 8192)
    cxb = [fb(12288, 512), fb(12800, 512)]
    xna = fb(0, 4096)
    xta = fb(4096, 8192)
    sqj = fb(12288, 4096)

    ps_sc = pt("ps_sc", [128, 2048])
    ps_tb = cm.enter_context(nc.psum_tensor("ps_tb", [128, 2048], BF16))
    ps_pv = pt("ps_pv", [128, 512])
    ps_pj = pt("ps_pj", [128, 512])

    def vecs(par):
        b = par * 96
        names = ["m8", "m8z", "cmp8", "j8"]
        d = {}
        o = b
        for n_ in names:
            d[n_] = VEC[:, o:o + 8]; o += 8
        for n_ in ["nlo", "nhi", "nt", "acc", "ge", "nge", "hih", "cfin",
                   "idx", "v409", "zc", "rz", "t1"]:
            d[n_] = VEC[:, o:o + 1]; o += 1
        d["geu"] = VECU[:, par * 2:par * 2 + 1]
        d["ngeu"] = VECU[:, par * 2 + 1:par * 2 + 2]
        return d
    lnv = {}
    o = 192
    for n_ in ["ssum", "mean", "var", "rstd", "nb", "m1", "m2", "d12", "p1", "p2"]:
        lnv[n_] = VEC[:, o:o + 1]; o += 1

    streams = {k: [] for k in ("te", "dve", "act", "sync", "gps")}
    SEMS = {k: sem("s_" + k) for k in ("te", "dve", "act")}
    C = {k: 0 for k in SEMS}
    NRING = 8
    RING = {c: [sem("r_%s%d" % (c, i)) for i in range(NRING)]
            for c in ("ld", "gd", "out")}
    ridx = {"ld": 0, "gd": 0, "out": 0}

    def em(k, fn, is_wait=False):
        if k in ("dve", "act") and not is_wait:
            streams[k].append(lambda e: e.drain())
        streams[k].append(fn)

    def wait(k, ctr, val):
        if val is None:
            return
        if isinstance(val, tuple):
            c_, slot, v = val
            em(k, lambda e, s_=RING[c_][slot], v=v: e.wait_ge(s_, v),
               is_wait=True)
            return
        if val <= 0:
            return
        em(k, lambda e, s_=SEMS[ctr], v=val: e.wait_ge(s_, v), is_wait=True)

    def inc(k, ctr, fn):
        if ctr in ("ld", "gd", "out"):
            j = ridx[ctr]
            ridx[ctr] += 1
            slot, rounds = j % NRING, j // NRING
            s_ = RING[ctr][slot]
            if rounds > 0:
                em(k, lambda e, s_=s_, v=rounds * 16: e.wait_ge(s_, v))
            em(k, lambda e, fn=fn, s_=s_: fn(e).then_inc(s_, 16))
            return (ctr, slot, (rounds + 1) * 16)
        C[ctr] += 1
        em(k, lambda e, fn=fn, s_=SEMS[ctr]: fn(e).then_inc(s_, 1))
        return C[ctr]

    def inclast(k, ctr):
        C[ctr] += 1
        fn_last = streams[k].pop()
        em(k, lambda e, fn=fn_last, s_=SEMS[ctr]: fn(e).then_inc(s_, 1),
           is_wait=True)
        return C[ctr]

    def dma_barrier(k, ctr):
        for slot in range(NRING):
            n_used = (ridx[ctr] - 1 - slot) // NRING + 1 if ridx[ctr] > slot else 0
            if n_used > 0:
                em(k, lambda e, s_=RING[ctr][slot], v=n_used * 16:
                   e.wait_ge(s_, v))

    ev = {k: {} for k in
          ["wdma", "wfree", "xwdma", "xwfree", "qk", "vready", "copy", "cnt",
           "upd", "exp", "phat", "scores", "round", "ptev", "pv", "cxe",
           "cxdma", "stt", "init", "pjfree_pj", "pjfree_ms"]}
    pend = []

    def flush(nmax=None):
        n = 0
        while pend and (nmax is None or n < nmax):
            pend.pop(0)(); n += 1

    ctoks = [
        inc("sync", "ld", lambda e: e.dma_start(out=bq_sb[:, :], in_=bqc)),
        inc("sync", "ld", lambda e: e.dma_start(out=bk_sb[:, :], in_=bkc)),
        inc("sync", "ld", lambda e: e.dma_start(out=bv_sb[:, :], in_=bvc)),
        inc("sync", "ld", lambda e: e.dma_start(out=coliota[:, :], in_=iotad)),
        inc("gps", "gd", lambda e: e.dma_start(out=bo_sb[:, :], in_=boc)),
        inc("gps", "gd", lambda e: e.dma_start(out=ident[:, :], in_=identd)),
        inc("gps", "gd", lambda e: e.dma_start(out=rept[:, :], in_=reptd)),
    ]
    for _t in ctoks:
        wait("te", None, _t)
        wait("dve", None, _t)
        wait("act", None, _t)
    em("dve", lambda e: e.memset(ones1[:, :], 1.0))

    ld_ev, a2_of, spill_ev, evac_ev, teA = {}, {}, {}, {}, {}
    for tsl in range(2):
        ld_ev[tsl] = inc("sync", "ld", lambda e, tsl=tsl: e.dma_start(
            out=xc[tsl % 2][:, :], in_=x[tsl * 128:(tsl + 1) * 128, :]))
    for ts in range(16):
        cb = xc[ts % 2]
        wait("act", "ld", ld_ev[ts])
        a1 = inc("act", "act", lambda e, cb=cb: e.activation(
            out=sqj[:, :], in_=cb, func=AF.Square, accum_out=lnv["ssum"]))
        wait("dve", "ld", ld_ev[ts])
        em("dve", lambda e, cb=cb: e.tensor_reduce(
            out=lnv["mean"], in_=cb, axis=mybir.AxisListType.X, op=AluOpType.add))
        em("dve", lambda e: e.tensor_scalar(
            out=lnv["mean"], in0=lnv["mean"], scalar1=1.0 / D, scalar2=None,
            op0=AluOpType.mult))
        wait("dve", "act", a1)
        em("dve", lambda e: e.tensor_tensor(
            out=lnv["var"], in0=lnv["mean"], in1=lnv["mean"], op=AluOpType.mult))
        em("dve", lambda e: e.scalar_tensor_tensor(
            out=lnv["var"], in0=lnv["ssum"], scalar=1.0 / D, in1=lnv["var"],
            op0=AluOpType.mult, op1=AluOpType.subtract))
        em("dve", lambda e: e.tensor_scalar(
            out=lnv["var"], in0=lnv["var"], scalar1=EPS, scalar2=None,
            op0=AluOpType.add))
        d0 = inc("dve", "dve", lambda e: e.reciprocal(
            out=lnv["rstd"], in_=lnv["var"]))
        wait("act", "dve", d0)
        a_s = inc("act", "act", lambda e: e.activation(
            out=lnv["rstd"], in_=lnv["rstd"], func=AF.Sqrt))
        wait("dve", "act", a_s)
        d1 = inc("dve", "dve", lambda e: e.scalar_tensor_tensor(
            out=lnv["nb"], in0=lnv["mean"], scalar=-1.0, in1=lnv["rstd"],
            op0=AluOpType.mult, op1=AluOpType.mult))
        wait("act", "dve", d1)
        if ts >= 1:
            wait("act", "te", teA[(ts - 1, 1)])
        a2 = inc("act", "act", lambda e, cb=cb: e.activation(
            out=xna[:, :], in_=cb, func=AF.Identity, bias=lnv["nb"], scale=lnv["rstd"]))
        a2_of[ts] = a2
        for rnd in range(2):
            wait("te", "act", a2)
            if (ts, rnd) != (0, 0):
                wait("te", "dve", evac_ev[(ts, 0) if rnd else (ts - 1, 1)])
            for k in range(16):
                dc = rnd * 16 + k
                em("te", lambda e, dc=dc, k=k: e.transpose(
                    out=ps_tb[:, k * 128:(k + 1) * 128],
                    in_=xna[:, dc * 128:(dc + 1) * 128], identity=ident[:, :]))
            t1 = inclast("te", "te")
            teA[(ts, rnd)] = t1
            wait("dve", "te", t1)
            if ts >= 2 and rnd == 0 and ts % 2 == 0:
                wait("dve", "ld", spill_ev[ts - 1])
            evac_ev[(ts, rnd)] = inc("dve", "dve", lambda e, rnd=rnd, ts=ts:
                e.tensor_copy(
                    out=xta[:, :].rearrange("p (dc c) -> p dc c", c=WIN)[
                        :, rnd * 16:(rnd + 1) * 16,
                        (ts % 2) * 128:(ts % 2) * 128 + 128],
                    in_=ps_tb[:, :].rearrange("p (k c) -> p k c", k=16)))
        if ts % 2 == 1:
            wait("sync", "dve", evac_ev[(ts, 1)])
            spill_ev[ts] = inc("sync", "ld", lambda e, ts=ts: e.dma_start(
                out=xnt[ts // 2], in_=xta[:, :]))
        if ts + 2 < 16:
            tsl = ts + 2
            wait("sync", "act", a2_of[tsl - 2])
            ld_ev[tsl] = inc("sync", "ld", lambda e, tsl=tsl: e.dma_start(
                out=xc[tsl % 2][:, :], in_=x[tsl * 128:(tsl + 1) * 128, :]))

    def emit_w_dma(h, widx):
        if (h - 1, widx) in ev["wfree"]:
            wait("gps", "te", ev["wfree"][(h - 1, widx)])
        src = [wqT, wkT, wvT][widx]
        ev["wdma"][(h, widx)] = inc("gps", "gd", lambda e, src=src, h=h, widx=widx:
            e.dma_start(out=wbuf[widx][:, :], in_=src[h]))

    def emit_xw_dma(h, w):
        par = w % 2
        key = (h, w - 2) if w - 2 >= 0 else (h - 1, NWIN - 2 + w)
        if key in ev["xwfree"]:
            wait("sync", "te", ev["xwfree"][key])
        if h == 0:
            wait("sync", None, spill_ev[w * 2 + 1])
        ev["xwdma"][(h, w)] = inc("sync", "ld", lambda e, w=w, par=par:
            e.dma_start(out=xw[par][:, :], in_=xnt[w]))

    def emit_proj_window(h, w):
        hp = h % 2
        par = w % 2
        wait("te", "ld", ev["xwdma"][(h, w)])
        if w == 0:
            for widx in range(3):
                wait("te", "gd", ev["wdma"][(h, widx)])
        segs = []
        if w * WIN < TOK:
            segs.append((ps_pj, "pjfree_pj", wbuf[0],
                         qT[hp][:, w * WIN:(w + 1) * WIN], bq_sb[:, h:h + 1], SQ))
        segs.append((ps_pj, "pjfree_pj", wbuf[1],
                     kT[hp][:, w * WIN:(w + 1) * WIN], bk_sb[:, h:h + 1], None))
        segs.append((ps_pj, "pjfree_pj", wbuf[2],
                     vTb[:, w * WIN:(w + 1) * WIN], bv_sb[:, h:h + 1], None))
        for ps, fk, wb, dst, bias, scl in segs:
            if 0 in ev[fk]:
                wait("te", "dve", ev[fk][0])
            for dc in range(NDC):
                em("te", lambda e, dc=dc, wb=wb, ps=ps, par=par: e.matmul(
                    out=ps[:, 0:WIN], lhsT=wb[:, dc * 128:(dc + 1) * 128],
                    rhs=xw[par][:, dc * WIN:(dc + 1) * WIN],
                    start=(dc == 0), stop=(dc == NDC - 1)))
            t = inclast("te", "te")
            wait("dve", "te", t)
            if scl is None:
                ev[fk][0] = inc("dve", "dve", lambda e, dst=dst, bias=bias, ps=ps:
                    e.tensor_scalar(out=dst, in0=ps[:, 0:WIN], scalar1=bias,
                                    scalar2=None, op0=AluOpType.add))
            else:
                ev[fk][0] = inc("dve", "dve",
                    lambda e, dst=dst, bias=bias, scl=scl, ps=ps:
                    e.tensor_scalar(out=dst, in0=ps[:, 0:WIN], scalar1=bias,
                                    scalar2=scl, op0=AluOpType.add,
                                    op1=AluOpType.mult))
        wait("te", "dve", ev["pjfree_pj"][0])
        if 0 in ev["pjfree_ms"]:
            wait("te", "dve", ev["pjfree_ms"][0])
        for kk in range(WIN // 128):
            kc = (w * WIN) // 128 + kk
            em("te", lambda e, kc=kc, kk=kk: e.transpose(
                out=ps_tb[:, 1024 + kk * 128:1024 + (kk + 1) * 128],
                in_=vTb[:, kc * 128:(kc + 1) * 128], identity=ident[:, :]))
        t2 = inclast("te", "te")
        wait("dve", "te", t2)
        ev["pjfree_ms"][0] = inc("dve", "dve", lambda e, w=w, hp=hp: e.tensor_copy(
            out=vnat[hp][:, w * WIN:(w + 1) * WIN], in_=ps_tb[:, 1024:1024 + WIN]))
        ev["xwfree"][(h, w)] = t2
        if w == NWIN - 1:
            ev["qk"][h] = ev["pjfree_pj"][0]
            ev["vready"][h] = ev["pjfree_ms"][0]
            for widx in range(3):
                ev["wfree"][(h, widx)] = t2

    def emit_scores(i, h, qt):
        hp = h % 2
        wait("te", "dve", ev["qk"][h])
        if i >= 1:
            wait("te", "act", ev["copy"][i - 1])
        for b_ in range(4):
            em("te", lambda e, b_=b_, hp=hp, qt=qt: e.matmul(
                out=ps_sc[:, b_ * 512:(b_ + 1) * 512],
                lhsT=qT[hp][:, qt * 128:(qt + 1) * 128],
                rhs=kT[hp][:, b_ * 512:(b_ + 1) * 512], start=True, stop=True))
        ev["scores"][i] = inclast("te", "te")

    def emit_tile_vec(i):
        par = i % 2
        V = vecs(par)
        s_ = sbufs[par]
        wait("act", "te", ev["scores"][i])
        if i >= 2:
            wait("act", "dve", ev["stt"][i - 2])
        ev["copy"][i] = inc("act", "act", lambda e, s_=s_: e.copy(
            out=s_[:, :], in_=ps_sc[:, :]))
        wait("dve", "act", ev["copy"][i])
        em("dve", lambda e, s_=s_, V=V: e.max(out=V["m8"], in_=s_))
        em("dve", lambda e, V=V: e.tensor_scalar(
            out=V["nlo"], in0=V["m8"][:, 0:1], scalar1=-2.0, scalar2=50.0,
            op0=AluOpType.mult, op1=AluOpType.add))
        em("dve", lambda e, V=V: e.tensor_scalar(
            out=V["nhi"], in0=V["m8"][:, 7:8], scalar1=-2.0, scalar2=None,
            op0=AluOpType.mult))
        em("dve", lambda e, V=V: e.tensor_scalar(
            out=V["t1"], in0=V["m8"][:, 0:1], scalar1=10.0, scalar2=None,
            op0=AluOpType.mult))
        em("dve", lambda e, V=V: e.scalar_tensor_tensor(
            out=V["nt"], in0=V["m8"][:, 7:8], scalar=-12.0, in1=V["t1"],
            op0=AluOpType.mult, op1=AluOpType.add))
        em("dve", lambda e, V=V: e.tensor_tensor(
            out=V["nt"], in0=V["nt"], in1=V["nhi"], op=AluOpType.max))
        ev["init"][i] = inc("dve", "dve", lambda e, V=V: e.tensor_tensor(
            out=V["nt"], in0=V["nt"], in1=V["nlo"], op=AluOpType.min))
        for j in range(NJ):
            wait("act", "dve", ev["init"][i] if j == 0 else ev["upd"][(i, j - 1)])
            ev["cnt"][(i, j)] = inc("act", "act", lambda e, s_=s_, V=V:
                e.activation(out=junkA[:, :], in_=s_, func=AF.Sign,
                             bias=V["nt"], scale=2.0, accum_out=V["acc"]))
            wait("dve", "act", ev["cnt"][(i, j)])
            em("dve", lambda e, V=V: e.tensor_scalar(
                out=V["geu"], in0=V["acc"], scalar1=float(2 * KTOP - S),
                scalar2=None, op0=AluOpType.is_ge))
            em("dve", lambda e, V=V: e.tensor_scalar(
                out=V["ngeu"], in0=V["acc"], scalar1=float(2 * KTOP - S),
                scalar2=None, op0=AluOpType.is_lt))
            em("dve", lambda e, V=V: e.copy_predicated(
                out=V["nlo"], mask=V["geu"], data=V["nt"]))
            em("dve", lambda e, V=V: e.copy_predicated(
                out=V["nhi"], mask=V["ngeu"], data=V["nt"]))
            em("dve", lambda e, V=V: e.tensor_tensor(
                out=V["nt"], in0=V["nlo"], in1=V["nhi"], op=AluOpType.add))
            ev["upd"][(i, j)] = inc("dve", "dve", lambda e, V=V: e.tensor_scalar(
                out=V["nt"], in0=V["nt"], scalar1=0.5, scalar2=None,
                op0=AluOpType.mult))
            flush(2)
        ev["exp"][i] = inc("act", "act", lambda e, s_=s_, par=par: e.activation(
            out=ebuf[par][:, :], in_=s_, func=AF.Exp))
        em("dve", lambda e, V=V: e.tensor_scalar(
            out=V["hih"], in0=V["nhi"], scalar1=-0.5, scalar2=None,
            op0=AluOpType.mult))
        em("dve", lambda e, s_=s_, V=V: e.tensor_scalar(
            out=junkD[:, :], in0=s_, scalar1=V["hih"], scalar2=0.0,
            op0=AluOpType.is_ge, op1=AluOpType.add, accum_out=V["cfin"]))
        em("dve", lambda e, s_=s_: e.scalar_tensor_tensor(
            out=zbuf[:, :], in0=junkD[:, :], scalar=-1.0e30, in1=s_,
            op0=AluOpType.mult, op1=AluOpType.add))
        em("dve", lambda e, V=V: e.max(out=V["m8z"], in_=zbuf[:, :]))
        em("dve", lambda e, V=V: e.tensor_scalar(
            out=V["idx"], in0=V["cfin"], scalar1=-1.0, scalar2=float(KTOP - 1),
            op0=AluOpType.mult, op1=AluOpType.add))
        em("dve", lambda e, V=V: e.tensor_scalar(
            out=V["idx"], in0=V["idx"], scalar1=0.0, scalar2=7.0,
            op0=AluOpType.max, op1=AluOpType.min))
        em("dve", lambda e, V=V: e.tensor_scalar(
            out=V["cmp8"], in0=coliota[:, :], scalar1=V["idx"], scalar2=None,
            op0=AluOpType.is_equal))
        em("dve", lambda e, V=V: e.tensor_tensor_reduce(
            out=V["j8"], in0=V["m8z"], in1=V["cmp8"], scale=1.0, scalar=0.0,
            op0=AluOpType.mult, op1=AluOpType.add, accum_out=V["v409"]))
        wait("dve", "act", ev["exp"][i])
        if i >= 2:
            wait("dve", "te", ev["round"][(i - 2, 1)])
        em("dve", lambda e, s_=s_, V=V, par=par: e.scalar_tensor_tensor(
            out=pbuf[par][:, :], in0=s_, scalar=V["v409"], in1=ebuf[par][:, :],
            op0=AluOpType.is_ge, op1=AluOpType.mult, accum_out=V["zc"]))
        em("dve", lambda e, V=V: e.reciprocal(out=V["rz"], in_=V["zc"]))
        pv_ = inc("dve", "dve", lambda e, V=V, par=par: e.tensor_scalar(
            out=pbuf[par][:, :], in0=pbuf[par][:, :], scalar1=V["rz"],
            scalar2=None, op0=AluOpType.mult))
        ev["phat"][i] = pv_
        ev["stt"][i] = pv_

    def emit_transposes_pv(i, h, qt):
        par = i % 2
        gq = qt % 4
        g = qt // 4
        gid = i // 4
        wait("te", "dve", ev["phat"][i])
        for rnd in range(2):
            if rnd == 1:
                wait("te", "dve", ev["ptev"][(i, 0)])
            elif i >= 1:
                wait("te", "dve", ev["ptev"][(i - 1, 1)])
            if rnd == 0 and (gid - 1) in ev["cxe"]:
                wait("te", "dve", ev["cxe"][gid - 1])
            for k in range(8):
                kc = rnd * 8 + k
                em("te", lambda e, kc=kc, k=k, par=par: e.transpose(
                    out=ps_tb[:, k * 128:(k + 1) * 128],
                    in_=pbuf[par][:, kc * 128:(kc + 1) * 128],
                    identity=ident[:, :]))
            t = inclast("te", "te")
            ev["round"][(i, rnd)] = t

            def evac(i=i, rnd=rnd, gq=gq, t=t):
                wait("dve", "te", t)
                ev["ptev"][(i, rnd)] = inc("dve", "dve", lambda e, rnd=rnd, gq=gq:
                    e.tensor_copy(
                        out=pTg[:, :].rearrange("p (kc c) -> p kc c", kc=16)[
                            :, rnd * 8:(rnd + 1) * 8, gq * 128:(gq + 1) * 128],
                        in_=ps_tb[:, 0:1024].rearrange("p (k c) -> p k c", k=8)))
            pend.append(evac)
            if rnd == 0:
                flush(len(pend))
        flush(len(pend))
        if gq == 3:
            hp = h % 2
            wait("te", "dve", ev["ptev"][(i, 1)])
            wait("te", "dve", ev["vready"][h])
            for kc in range(16):
                em("te", lambda e, kc=kc, hp=hp: e.matmul(
                    out=ps_pv[:, 0:512],
                    lhsT=vnat[hp][:, kc * 128:(kc + 1) * 128],
                    rhs=pTg[:, kc * 512:(kc + 1) * 512],
                    start=(kc == 0), stop=(kc == 15)))
            t = inclast("te", "te")
            ev["pv"][gid] = t

            def evacx(gid=gid, g=g, h=h, t=t):
                wait("dve", "te", t)
                if h >= 1:
                    wait("dve", "ld", ev["cxdma"].get(h - 1))
                ev["cxe"][gid] = inc("dve", "dve", lambda e, g=g: e.tensor_copy(
                    out=cxb[g][:, :], in_=ps_pv[:, 0:512]))
            pend.append(evacx)
            flush(len(pend))
            if g == 1:
                wait("sync", "dve", ev["cxe"][gid])
                ev["cxdma"][h] = inc("sync", "ld", lambda e, h=h: e.dma_start(
                    out=ctxd[h], in_=BFb[:, 12288:12288 + 1024]))

    if stop == "A":
        nheads = 0
    if nheads > 0:
        for widx in range(3):
            emit_w_dma(0, widx)
        emit_xw_dma(0, 0)
        emit_xw_dma(0, 1)
        for w in range(NWIN):
            if w >= 2:
                emit_xw_dma(0, w)
            emit_proj_window(0, w)
    for h in range(nheads):
        if h + 1 < nheads:
            for widx in range(3):
                emit_w_dma(h + 1, widx)
            emit_xw_dma(h + 1, 0)
            emit_xw_dma(h + 1, 1)
        if h + 1 >= nheads:
            pass
        for qt in range(NQT):
            i = h * NQT + qt
            emit_scores(i, h, qt)
            emit_tile_vec(i)
            if i >= 1:
                emit_transposes_pv(i - 1, (i - 1) // NQT, (i - 1) % NQT)
            if h + 1 < nheads:
                if qt >= 2:
                    emit_xw_dma(h + 1, qt)
                emit_proj_window(h + 1, qt)
    if nheads > 0:
        emit_transposes_pv(nheads * NQT - 1, nheads - 1, NQT - 1)
    flush()

    if stop in ("A", "attn"):
        _finish(nc, streams, cm)
        return nc
    if ev["pv"]:
        wait("gps", "te", ev["pv"][max(ev["pv"])])
    for oc in range(NH):
        if oc in ev["cxdma"]:
            wait("sync", None, ev["cxdma"][oc])
        inc("sync", "ld", lambda e, oc=oc: e.dma_start(
            out=ctx[:, oc * 1024:(oc + 1) * 1024], in_=ctxd[oc]))
    psl = [ps_sc[:, i * 512:(i + 1) * 512] for i in range(4)]
    wo_ev, ev_ev = {}, {}
    seq = 0
    for nc_ in range(8):
        for grp in range(2):
            tss = [grp * 4 + t for t in range(4)]
            for oc in range(32):
                par = seq % 2
                if seq >= 2:
                    wait("gps", "te", wo_ev[seq - 2])
                gv = inc("gps", "gd", lambda e, oc=oc, nc_=nc_, par=par: e.dma_start(
                    out=ws[par][:, :], in_=woT[oc, nc_]))
                wait("te", "gd", gv)
                if oc == 0:
                    if nc_ == 0 and grp == 0:
                        dma_barrier("te", "ld")
                    else:
                        prv = (nc_, grp - 1) if grp == 1 else (nc_ - 1, 1)
                        for t in range(4):
                            wait("te", "dve", ev_ev[(prv, t)])
                for t, ts in enumerate(tss):
                    em("te", lambda e, t=t, ts=ts, oc=oc, par=par: e.matmul(
                        out=psl[t],
                        lhsT=ctx[:, oc * 1024 + ts * 128:oc * 1024 + (ts + 1) * 128],
                        rhs=ws[par][:, :], start=(oc == 0), stop=False))
                wo_ev[seq] = inclast("te", "te")
                seq += 1
            for t in range(4):
                em("te", lambda e, t=t, nc_=nc_: e.matmul(
                    out=psl[t], lhsT=ones1[0:1, :],
                    rhs=bo_sb[0:1, nc_ * 512:(nc_ + 1) * 512], start=False,
                    stop=True))
            tvb = inclast("te", "te")
            for t, ts in enumerate(tss):
                par = ts % 2
                if ("cx", par) in ev:
                    wait("sync", "dve", ev[("cx", par)])
                xv = inc("sync", "ld", lambda e, ts=ts, nc_=nc_, par=par:
                         e.dma_start(
                    out=xs_c[par][:, :],
                    in_=x[ts * 128:(ts + 1) * 128, nc_ * 512:(nc_ + 1) * 512]))
                wait("dve", "ld", xv)
                wait("dve", "te", tvb)
                if ("cst", par) in ev:
                    wait("dve", None, ev[("cst", par)])
                dv = inc("dve", "dve", lambda e, t=t, par=par: e.tensor_tensor(
                    out=om[par][:, :], in0=psl[t], in1=xs_c[par][:, :],
                    op=AluOpType.add))
                ev_ev[((nc_, grp), t)] = dv
                ev[("cx", par)] = dv
                wait("sync", "dve", dv)
                ev[("cst", par)] = inc("sync", "ld",
                    lambda e, ts=ts, nc_=nc_, par=par: e.dma_start(
                    out=xmidd[ts, :, nc_ * 512:(nc_ + 1) * 512], in_=om[par][:, :]))
    if stop == "C":
        _finish(nc, streams, cm)
        return nc

    g1 = inc("gps", "gd", lambda e: e.dma_start(out=gwb[:, :], in_=gWT))
    xm_ev, a2d, trd, trte = {}, {}, {}, {}
    gr_last = 0
    dma_barrier("sync", "ld")
    for tsl in range(2):
        xm_ev[tsl] = inc("sync", "ld", lambda e, tsl=tsl: e.dma_start(
            out=xmc[tsl % 2][:, :], in_=xmidd[tsl]))
    for ts in range(8):
        cb = xmc[ts % 2]
        wait("act", "ld", xm_ev[ts])
        a1 = inc("act", "act", lambda e, cb=cb: e.activation(
            out=BFb[:, 0:4096], in_=cb, func=AF.Square, accum_out=lnv["ssum"]))
        wait("dve", "ld", xm_ev[ts])
        em("dve", lambda e, cb=cb: e.tensor_reduce(
            out=lnv["mean"], in_=cb, axis=mybir.AxisListType.X, op=AluOpType.add))
        em("dve", lambda e: e.tensor_scalar(
            out=lnv["mean"], in0=lnv["mean"], scalar1=1.0 / D, scalar2=None,
            op0=AluOpType.mult))
        wait("dve", "act", a1)
        em("dve", lambda e: e.tensor_tensor(
            out=lnv["var"], in0=lnv["mean"], in1=lnv["mean"], op=AluOpType.mult))
        em("dve", lambda e: e.scalar_tensor_tensor(
            out=lnv["var"], in0=lnv["ssum"], scalar=1.0 / D, in1=lnv["var"],
            op0=AluOpType.mult, op1=AluOpType.subtract))
        em("dve", lambda e: e.tensor_scalar(
            out=lnv["var"], in0=lnv["var"], scalar1=EPS, scalar2=None,
            op0=AluOpType.add))
        d0 = inc("dve", "dve", lambda e: e.reciprocal(
            out=lnv["rstd"], in_=lnv["var"]))
        wait("act", "dve", d0)
        a_s = inc("act", "act", lambda e: e.activation(
            out=lnv["rstd"], in_=lnv["rstd"], func=AF.Sqrt))
        wait("dve", "act", a_s)
        d1 = inc("dve", "dve", lambda e: e.scalar_tensor_tensor(
            out=lnv["nb"], in0=lnv["mean"], scalar=-1.0, in1=lnv["rstd"],
            op0=AluOpType.mult, op1=AluOpType.mult))
        wait("act", "dve", d1)
        if ts >= 1:
            wait("act", "te", trte[(ts - 1, 1)])
        a2 = inc("act", "act", lambda e, cb=cb: e.activation(
            out=xn2[:, :], in_=cb, func=AF.Identity, bias=lnv["nb"], scale=lnv["rstd"]))
        a2d[ts] = a2
        for rnd in range(2):
            wait("te", "act", a2)
            if (ts, rnd) != (0, 0):
                wait("te", "dve", trd[(ts, 0) if rnd else (ts - 1, 1)])
            if rnd == 0 and gr_last:
                wait("te", "dve", gr_last)
            for k in range(16):
                dc = rnd * 16 + k
                em("te", lambda e, dc=dc, k=k: e.transpose(
                    out=ps_tb[:, k * 128:(k + 1) * 128],
                    in_=xn2[:, dc * 128:(dc + 1) * 128], identity=ident[:, :]))
            t1 = inclast("te", "te")
            trte[(ts, rnd)] = t1
            wait("dve", "te", t1)
            trd[(ts, rnd)] = inc("dve", "dve", lambda e, rnd=rnd, ts=ts:
                e.tensor_copy(
                    out=ctx[:, :].rearrange("p (dc c) -> p dc c", dc=32)[
                        :, rnd * 16:(rnd + 1) * 16, ts * 128:(ts + 1) * 128],
                    in_=ps_tb[:, :].rearrange("p (k c) -> p k c", k=16)))
        wait("te", "gd", g1)
        wait("te", "dve", trd[(ts, 1)])
        if ("lg",) in ev:
            wait("te", "dve", ev[("lg",)])
        for dc in range(NDC):
            em("te", lambda e, dc=dc, ts=ts: e.matmul(
                out=ps_pj[:, 0:28],
                lhsT=ctx[:, dc * 1024 + ts * 128:dc * 1024 + (ts + 1) * 128],
                rhs=gwb[:, dc * 28:(dc + 1) * 28],
                start=(dc == 0), stop=(dc == NDC - 1)))
        t2 = inclast("te", "te")
        wait("dve", "te", t2)
        ev[("lg",)] = inc("dve", "dve", lambda e: e.tensor_copy(
            out=lgs[:, :], in_=ps_pj[:, 0:28]))
        em("dve", lambda e: e.tensor_reduce(
            out=lnv["m1"], in_=lgs[:, :], axis=mybir.AxisListType.X,
            op=AluOpType.max))
        em("dve", lambda e: e.tensor_scalar(
            out=e1s[:, :], in0=lgs[:, :], scalar1=lnv["m1"], scalar2=None,
            op0=AluOpType.is_ge))
        em("dve", lambda e: e.scalar_tensor_tensor(
            out=l2s[:, :], in0=e1s[:, :], scalar=-1.0e30, in1=lgs[:, :],
            op0=AluOpType.mult, op1=AluOpType.add))
        em("dve", lambda e: e.tensor_reduce(
            out=lnv["m2"], in_=l2s[:, :], axis=mybir.AxisListType.X,
            op=AluOpType.max))
        em("dve", lambda e: e.tensor_scalar(
            out=e2s[:, :], in0=l2s[:, :], scalar1=lnv["m2"], scalar2=None,
            op0=AluOpType.is_ge))
        d2 = inc("dve", "dve", lambda e: e.tensor_tensor(
            out=lnv["d12"], in0=lnv["m1"], in1=lnv["m2"], op=AluOpType.subtract))
        wait("act", "dve", d2)
        a3 = inc("act", "act", lambda e: e.activation(
            out=lnv["p1"], in_=lnv["d12"], func=AF.Sigmoid))
        wait("dve", "act", a3)
        em("dve", lambda e: e.tensor_scalar(
            out=lnv["p2"], in0=lnv["p1"], scalar1=-1.0, scalar2=1.0,
            op0=AluOpType.mult, op1=AluOpType.add))
        em("dve", lambda e: e.tensor_scalar(
            out=gfs[:, :], in0=e1s[:, :], scalar1=lnv["p1"], scalar2=None,
            op0=AluOpType.mult))
        em("dve", lambda e: e.scalar_tensor_tensor(
            out=gfs[:, :], in0=e2s[:, :], scalar=lnv["p2"], in1=gfs[:, :],
            op0=AluOpType.mult, op1=AluOpType.add))
        d3 = inc("dve", "dve", lambda e: e.tensor_copy(out=gbb[:, :], in_=gfs[:, :]))
        wait("te", "dve", d3)
        em("te", lambda e: e.transpose(
            out=ps_tb[0:28, 0:128], in_=gbb[:, :], identity=ident[:, :]))
        t3 = inclast("te", "te")
        wait("dve", "te", t3)
        d4 = inc("dve", "dve", lambda e: e.tensor_copy(
            out=gtb[0:28, :], in_=ps_tb[0:28, 0:128]))
        wait("te", "dve", d4)
        for ec in range(2):
            if gr_last:
                wait("te", "dve", gr_last)
            em("te", lambda e, ec=ec: e.matmul(
                out=ps_pv[:, 0:128], lhsT=rept[:, ec * 128:(ec + 1) * 128],
                rhs=gtb[0:28, :], start=True, stop=True))
            t4 = inclast("te", "te")
            wait("dve", "te", t4)
            gr_last = inc("dve", "dve", lambda e, ec=ec, ts=ts: e.tensor_copy(
                out=gr[:, ec * 1024 + ts * 128:ec * 1024 + (ts + 1) * 128],
                in_=ps_pv[:, 0:128]))
        if ts + 2 < 8:
            tsl = ts + 2
            wait("sync", "act", a2d[tsl - 2])
            xm_ev[tsl] = inc("sync", "ld", lambda e, tsl=tsl: e.dma_start(
                out=xmc[tsl % 2][:, :], in_=xmidd[tsl]))
    if stop == "D1":
        _finish(nc, streams, cm)
        return nc
    la_ev, hT_ev = {}, {}
    seqa = 0
    prev_hT = None
    for ec in range(2):
        for half in range(2):
            for dc in range(NDC):
                par = seqa % 2
                if seqa >= 2:
                    wait("gps", "te", la_ev[seqa - 2])
                gv = inc("gps", "gd", lambda e, ec=ec, dc=dc, par=par: e.dma_start(
                    out=lab[par][:, :], in_=lAT[ec, dc]))
                wait("te", "gd", gv)
                if dc == 0 and prev_hT is not None:
                    wait("te", "dve", prev_hT)
                em("te", lambda e, ec=ec, half=half, dc=dc, par=par: e.matmul(
                    out=ps_pj[:, :], lhsT=lab[par][:, :],
                    rhs=ctx[:, dc * 1024 + half * 512:dc * 1024 + (half + 1) * 512],
                    start=(dc == 0), stop=(dc == NDC - 1)))
                la_ev[seqa] = inclast("te", "te")
                seqa += 1
            wait("dve", "te", la_ev[seqa - 1])
            hT_ev[(ec, half)] = inc("dve", "dve", lambda e, ec=ec, half=half:
                e.tensor_tensor(
                    out=hT[:, ec * 1024 + half * 512:
                           ec * 1024 + (half + 1) * 512],
                    in0=ps_pj[:, :],
                    in1=gr[:, ec * 1024 + half * 512:
                           ec * 1024 + (half + 1) * 512],
                    op=AluOpType.mult))
            prev_hT = hT_ev[(ec, half)]
    if stop == "D2":
        _finish(nc, streams, cm)
        return nc
    bw_ev, lb_ev, fin_ev = {}, {}, {}
    seqb = 0
    for nc_ in range(8):
        for grp in range(2):
            tss = [grp * 4 + t for t in range(4)]
            for dc in range(32):
                par = seqb % 2
                if seqb >= 2:
                    wait("gps", "te", bw_ev[seqb - 2])
                gv = inc("gps", "gd", lambda e, dc=dc, nc_=nc_, par=par:
                         e.dma_start(out=ws[par][:, :], in_=bWT[dc, nc_]))
                wait("te", "gd", gv)
                if dc == 0:
                    if nc_ == 0 and grp == 0:
                        wait("te", "dve", hT_ev[(1, 1)])
                    else:
                        prv = (nc_, 0) if grp == 1 else (nc_ - 1, 1)
                        for t in range(4):
                            wait("te", "dve", fin_ev[(prv, t)])
                for t, ts in enumerate(tss):
                    em("te", lambda e, t=t, ts=ts, dc=dc, par=par: e.matmul(
                        out=psl[t],
                        lhsT=ctx[:, dc * 1024 + ts * 128:dc * 1024 + (ts + 1) * 128],
                        rhs=ws[par][:, :], start=(dc == 0), stop=False))
                bw_ev[seqb] = inclast("te", "te")
                seqb += 1
            for ec in range(2):
                par = ec
                if not (nc_ == 0 and grp == 0):
                    wait("gps", "te", lb_ev.get((nc_, grp - 1, ec)) or
                         lb_ev.get((nc_ - 1, 1, ec)))
                gv = inc("gps", "gd", lambda e, ec=ec, nc_=nc_, par=par:
                         e.dma_start(out=lbb[par][:, :], in_=lBT[ec, :, nc_]))
                wait("te", "gd", gv)
                for t, ts in enumerate(tss):
                    em("te", lambda e, t=t, ts=ts, ec=ec, par=par: e.matmul(
                        out=psl[t],
                        lhsT=hT[:, ec * 1024 + ts * 128:
                                ec * 1024 + (ts + 1) * 128],
                        rhs=lbb[par][:, :], start=False, stop=(ec == 1)))
                lb_ev[(nc_, grp, ec)] = inclast("te", "te")
            tvb = lb_ev[(nc_, grp, 1)]
            for t, ts in enumerate(tss):
                par = ts % 2
                if ("dx", par) in ev:
                    wait("sync", "dve", ev[("dx", par)])
                xv = inc("sync", "ld", lambda e, ts=ts, nc_=nc_, par=par:
                         e.dma_start(out=xr[par][:, :],
                                     in_=xmidd[ts, :, nc_ * 512:(nc_ + 1) * 512]))
                wait("dve", "ld", xv)
                wait("dve", "te", tvb)
                if ("dst", par) in ev:
                    wait("dve", "out", ev[("dst", par)])
                dv = inc("dve", "dve", lambda e, t=t, par=par: e.tensor_tensor(
                    out=om[par][:, :], in0=psl[t], in1=xr[par][:, :],
                    op=AluOpType.add))
                fin_ev[((nc_, grp), t)] = dv
                ev[("dx", par)] = dv
                wait("sync", "dve", dv)
                ev[("dst", par)] = inc("sync", "out",
                    lambda e, ts=ts, nc_=nc_, par=par: e.dma_start(
                    out=out[ts * 128:(ts + 1) * 128, nc_ * 512:(nc_ + 1) * 512],
                    in_=om[par][:, :]))
    dma_barrier("sync", "out")

    _finish(nc, streams, cm)
    return nc


def _finish(nc, streams, cm):
    with nc.Block() as block:
        @block.tensor
        def _(e):
            for f in streams["te"]:
                f(e)

        @block.vector
        def _(e):
            for f in streams["dve"]:
                f(e)

        @block.scalar
        def _(e):
            for f in streams["act"]:
                f(e)

        @block.sync
        def _(e):
            for f in streams["sync"]:
                f(e)

        @block.gpsimd
        def _(e):
            for f in streams["gps"]:
                f(e)
    cm.close()
    from concourse.library_overlay import lower_extended_insts
    lower_extended_insts(nc)


def kernel(L0_attn_hidden_states, L0_attn_wq, L0_attn_bq, L0_attn_wk, L0_attn_bk,
           L0_attn_wv, L0_attn_bv, L0_attn_wo, L0_attn_bo, L0_ffn_base_W,
           L0_ffn_gate_W, L0_ffn_lora_A, L0_ffn_lora_B):
    f32 = lambda a: np.ascontiguousarray(np.asarray(a, dtype=np.float32))
    xin = f32(L0_attn_hidden_states)
    wq, wk, wv, wo = (f32(L0_attn_wq), f32(L0_attn_wk), f32(L0_attn_wv),
                      f32(L0_attn_wo))
    bq, bk, bv, bo = (f32(L0_attn_bq), f32(L0_attn_bk), f32(L0_attn_bv),
                      f32(L0_attn_bo))
    bW, gW = f32(L0_ffn_base_W), f32(L0_ffn_gate_W)
    lA, lB = f32(L0_ffn_lora_A), f32(L0_ffn_lora_B)

    def tile_qkv(w):
        wt = np.ascontiguousarray(w.T)
        return np.ascontiguousarray(
            wt.reshape(NDC, 128, NH, 128).transpose(2, 1, 0, 3)
            .reshape(NH, 128, NDC * 128))
    wqT, wkT, wvT = tile_qkv(wq), tile_qkv(wk), tile_qkv(wv)
    woT = np.ascontiguousarray(wo.T.reshape(32, 128, 8, 512).transpose(0, 2, 1, 3))
    bWT = np.ascontiguousarray(bW.T.reshape(32, 128, 8, 512).transpose(0, 2, 1, 3))
    gWT = np.ascontiguousarray(
        gW.T.reshape(NDC, 128, 28).transpose(1, 0, 2).reshape(128, NDC * 28))
    lAT_ = lA.transpose(2, 0, 1).reshape(NDC, 128, 2, 112).transpose(2, 0, 1, 3)
    lAT = np.zeros((2, NDC, 128, 128), np.float32)
    lAT[:, :, :, :112] = lAT_
    lBT_ = lB.transpose(0, 2, 1).reshape(224, 4096).reshape(2, 112, 8, 512)
    lBT = np.zeros((2, 128, 8, 512), np.float32)
    lBT[:, :112] = lBT_
    bqc = np.ascontiguousarray(bq.reshape(NH, 128).T)
    bkc = np.ascontiguousarray(bk.reshape(NH, 128).T)
    bvc = np.ascontiguousarray(bv.reshape(NH, 128).T)
    boc = np.ascontiguousarray(bo.reshape(1, D))
    identd = np.eye(128, dtype=np.float32)
    reptd = np.zeros((28, 256), dtype=np.float32)
    for e_ in range(28):
        er = e_ * 8
        reptd[e_, (er // 112) * 128 + er % 112:
              (er // 112) * 128 + er % 112 + 8] = 2.0
    iotad = np.tile(np.arange(8, dtype=np.float32)[None, :], (128, 1))

    nc = build_nc()
    in_maps = []
    for core in range(8):
        b, hf = core // 2, core % 2
        xb = xin[b]
        if hf == 1:
            xb = np.concatenate([xb[TOK:], xb[:TOK]], axis=0)
        in_maps.append(dict(
            x=np.ascontiguousarray(xb), wqT=wqT, wkT=wkT, wvT=wvT, woT=woT,
            bWT=bWT, gWT=gWT, lAT=lAT, lBT=lBT, bqc=bqc, bkc=bkc, bvc=bvc,
            boc=boc, identd=identd, reptd=reptd, iotad=iotad))
    res = run_bass_kernel_spmd(nc, in_maps, core_ids=list(range(8)))
    outp = np.empty((4, S, D), dtype=np.float32)
    for core in range(8):
        b, hf = core // 2, core % 2
        outp[b, hf * TOK:(hf + 1) * TOK] = res.results[core]["out"]
    return outp
